# version 01
# baseline (speedup 1.0000x reference)
import numpy as np
import ml_dtypes
from contextlib import ExitStack

import concourse.bass as bass
import concourse.bacc as bacc
import concourse.mybir as mybir
from concourse.bass_utils import run_bass_kernel_spmd
from concourse.library_config import mlp as mlp_lib

F32 = mybir.dt.float32
BF16 = mybir.dt.bfloat16
I16 = mybir.dt.int16
AF = mybir.ActivationFunctionType
ALU = mybir.AluOpType
AX = mybir.AxisListType

NCORES = 8
T = 8192
D = 1024
NT = 64
INTER = 2048
NE = 16384
HEADS = 16
MAGIC = 1.5 * 2 ** 23
QSCALE = 2 ** 12
SOFF = 2048.0

SIGMA = np.array([16 * (v % 8) + v // 8 for v in range(128)])


def _consts():
    c = {}
    c["cIDENT"] = np.eye(128).astype(ml_dtypes.bfloat16)
    esig = np.zeros((128, 128), np.float32)
    esig[SIGMA, np.arange(128)] = 1.0
    c["cESIG"] = esig.astype(ml_dtypes.bfloat16)
    psig = np.zeros((128, 128), np.float32)
    psig[np.arange(128), SIGMA] = 1.0
    c["cPSIG"] = psig
    c["cMDIAG"] = np.tile(np.eye(128, dtype=np.float32), (1, 4)).astype(ml_dtypes.bfloat16)
    f = np.arange(128, dtype=np.float32)
    c["cIOTAOFFC"] = np.tile(127.0 - f + SOFF * 128.0, (128, 8)).reshape(128, 1024).astype(np.float32)
    col = np.arange(32)
    c["cIOTAI4"] = np.tile((63.0 - 8.0 * (col % 8)).astype(np.float32), (128, 1))
    c["cIOTAJ4"] = np.tile((-(col % 8)).astype(np.float32), (128, 1))
    c["cIOTA8C"] = np.tile(np.arange(8, dtype=np.float32), (128, 1))
    return c


def build(debug=(), phases="0ABCDPEFG"):
    nc = bacc.Bacc("TRN2")

    def dram(name, shape, dtype, kind="ExternalInput"):
        return nc.dram_tensor(name, shape, dtype, kind=kind).ap()

    x = dram("x", [T, D], F32)
    w1t = dram("w1t", [D, 3328], F32)
    wpt = dram("wpt", [D, D], F32)
    wgut = dram("wgut", [D, 2 * INTER], F32)
    wdt = dram("wdt", [INTER, D], F32)
    keysl = dram("keysl", [4, 64, 128], F32)
    kbd = dram("kbd", [2, 128, 512], F32)
    de = dram("de", [NE, D], F32)
    ue = dram("ue", [NE, D], F32)
    consts = {nm: dram(nm, list(ar.shape), BF16 if ar.dtype == ml_dtypes.bfloat16 else F32)
              for nm, ar in _consts().items()}
    out = dram("out", [T, D], F32, kind="ExternalOutput")

    def scratch(name, shape, dtype):
        kind = "ExternalOutput" if name in debug else "Internal"
        return dram(name, shape, dtype, kind=kind)

    xnb = scratch("xnb", [T, D], BF16)
    qkvT = scratch("qkvT", [3328, T], BF16)
    aT = scratch("aT", [D, T], BF16)
    sT = scratch("sT", [D, T], BF16)
    mlpT = scratch("mlpT", [D, T], BF16)
    es_sc = scratch("es_sc", [T, D], BF16)
    de_bf = scratch("de_bf", [NE, D], BF16)
    ue_bf = scratch("ue_bf", [NE, D], BF16)
    idx_sc = scratch("idx_sc", [NT, 4096], I16)
    if "cdbg" in debug:
        cdbg = {nm: dram("cdbg_" + nm, [NT, 128, w], F32, kind="ExternalOutput")
                for nm, w in [("P8", 64), ("Ix", 64), ("Sq", 64), ("K8", 32), ("Kq", 32),
                              ("M8", 32), ("I8", 32), ("J8", 32), ("E", 32), ("G1", 32)]}

    st = ExitStack()

    def sb(name, shape, dtype):
        return st.enter_context(nc.sbuf_tensor(name, shape, dtype))

    sIDENT = sb("sIDENT", [128, 128], BF16)
    sESIG = sb("sESIG", [128, 128], BF16)
    sPSIG = sb("sPSIG", [128, 128], F32)
    sMDIAG = sb("sMDIAG", [128, 512], BF16)
    sIOTAOFFC = sb("sIOTAOFFC", [128, 1024], F32)
    sIOTAI4 = sb("sIOTAI4", [128, 32], F32)
    sIOTAJ4 = sb("sIOTAJ4", [128, 32], F32)
    sIOTA8C = sb("sIOTA8C", [128, 8], F32)
    skbd = sb("skbd", [128, 1024], BF16)
    sgsig = sb("sgsig", [128, NT * 32], F32)
    sconst_map = {"cIDENT": sIDENT, "cESIG": sESIG, "cPSIG": sPSIG, "cMDIAG": sMDIAG,
                  "cIOTAOFFC": sIOTAOFFC, "cIOTAI4": sIOTAI4, "cIOTAJ4": sIOTAJ4,
                  "cIOTA8C": sIOTA8C}

    sem = {}
    cnt = {}
    for nm in ["ldS", "ldG", "ldT", "gat", "pe", "ve", "ac", "stS", "stG", "bar", "veg"]:
        sem[nm] = st.enter_context(nc.semaphore(nm))
        cnt[nm] = 0

    barv = [0]
    vegn = [0]

    def vegadv():
        vegn[0] += 1
        return vegn[0]

    def barrier(block):
        tgt = barv[0] + 5

        @block.sync
        def _(e):
            e.wait_ge(sem["ldS"], cnt["ldS"])
            e.wait_ge(sem["stS"], cnt["stS"])
            e.wait_ge(sem["ldT"], cnt["ldT"])
            e.sem_inc(sem["bar"], 1)
            e.wait_ge(sem["bar"], tgt)

        @block.gpsimd
        def _(e):
            e.wait_ge(sem["ldG"], cnt["ldG"])
            e.wait_ge(sem["stG"], cnt["stG"])
            e.wait_ge(sem["gat"], cnt["gat"])
            e.sem_inc(sem["bar"], 1)
            e.wait_ge(sem["bar"], tgt)

        for eng in ("tensor", "vector", "scalar"):
            def _f(e):
                e.sem_inc(sem["bar"], 1)
                e.wait_ge(sem["bar"], tgt)
            getattr(block, eng)(_f)
        barv[0] = tgt

    with nc.Block() as block:
        ph0 = sb("ph0_cv", [128, 2 * D], BF16)

        @block.sync
        def _(e):
            for nm, sbuf in sconst_map.items():
                cnt["ldS"] += 16
                e.dma_start(out=sbuf[:], in_=consts[nm]).then_inc(sem["ldS"], 16)

        @block.gpsimd
        def _(e):
            e.load_library(mlp_lib)
            for c in range(2):
                cnt["ldG"] += 16
                e.dma_start(out=skbd[:, 512 * c:512 * (c + 1)],
                            in_=kbd[c]).then_inc(sem["ldG"], 16)
            for ti, (src, dst) in enumerate([(de, de_bf), (ue, ue_bf)]):
                for i in range(128):
                    g = ti * 128 + i
                    par = g % 2
                    if g >= 2:
                        e.wait_ge(sem["stG"], 16 * (g - 1))
                    cnt["ldG"] += 16
                    e.dma_start(out=ph0[:, par * D:(par + 1) * D],
                                in_=src[128 * i:128 * (i + 1), :]).then_inc(sem["ldG"], 16)
                    e.wait_ge(sem["ldG"], cnt["ldG"])
                    cnt["stG"] += 16
                    e.dma_start(out=dst[128 * i:128 * (i + 1), :],
                                in_=ph0[:, par * D:(par + 1) * D]).then_inc(sem["stG"], 16)

        barrier(block)

        if "A" in phases:
         with ExitStack() as stA:
            sx = [stA.enter_context(nc.sbuf_tensor(f"A_x{i}", [128, D], F32)) for i in range(2)]
            sxn = [stA.enter_context(nc.sbuf_tensor(f"A_xn{i}", [128, D], BF16)) for i in range(2)]
            sjunk = stA.enter_context(nc.sbuf_tensor("A_junk", [128, D], BF16))
            sstat = [stA.enter_context(nc.sbuf_tensor(f"A_st{i}", [128, 8], F32)) for i in range(2)]

            ldS0, ve0, ac0, stG0 = cnt["ldS"], cnt["ve"], cnt["ac"], cnt["stG"]

            @block.sync
            def _(e):
                for i in range(NT):
                    if i >= 2:
                        e.wait_ge(sem["ac"], ac0 + 2 * i - 2)
                    cnt["ldS"] += 16
                    e.dma_start(out=sx[i % 2][:], in_=x[128 * i:128 * (i + 1), :]).then_inc(sem["ldS"], 16)

            @block.vector
            def _(e):
                for i in range(NT):
                    vb = ve0 + 2 * i
                    e.wait_ge(sem["ldS"], ldS0 + 16 * (i + 1))
                    if i >= 2:
                        e.wait_ge(sem["ac"], ac0 + 2 * i - 2)
                    xi = sx[i % 2]
                    stt = sstat[i % 2]
                    ssq, ssum, negmu, musq, var, sd, rstd, nmr = (stt[:, k:k + 1] for k in range(8))
                    e.tensor_tensor_reduce(out=sjunk[:], in0=xi[:], in1=xi[:], scale=1.0,
                                           scalar=float(D) * 1e-5, op0=ALU.mult, op1=ALU.add,
                                           accum_out=ssq)
                    e.reduce_sum(ssum, xi[:], axis=AX.X).then_inc(sem["veg"], 1)
                    e.wait_ge(sem["veg"], vegadv())
                    e.tensor_scalar_mul(negmu, ssum, -1.0 / D).then_inc(sem["veg"], 1)
                    e.wait_ge(sem["veg"], vegadv())
                    e.tensor_mul(musq, negmu, negmu).then_inc(sem["veg"], 1)
                    e.wait_ge(sem["veg"], vegadv())
                    e.scalar_tensor_tensor(out=var, in0=ssq, scalar=1.0 / D,
                                           op0=ALU.mult, in1=musq, op1=ALU.subtract).then_inc(sem["ve"], 1)
                    e.wait_ge(sem["ac"], ac0 + 2 * i + 1)
                    e.reciprocal(rstd, sd).then_inc(sem["veg"], 1)
                    e.wait_ge(sem["veg"], vegadv())
                    e.tensor_mul(nmr, negmu, rstd).then_inc(sem["ve"], 1)
            cnt["ve"] += 2 * NT

            @block.scalar
            def _(e):
                for i in range(NT):
                    e.wait_ge(sem["ve"], ve0 + 2 * i + 1)
                    if i >= 2:
                        e.wait_ge(sem["stG"], stG0 + 16 * (i - 1))
                    stt = sstat[i % 2]
                    var, sd, rstd, nmr = (stt[:, k:k + 1] for k in (4, 5, 6, 7))
                    e.activation(out=sd, in_=var, func=AF.Sqrt).then_inc(sem["ac"], 1)
                    e.wait_ge(sem["ve"], ve0 + 2 * i + 2)
                    e.activation(out=sxn[i % 2][:], in_=sx[i % 2][:], func=AF.Identity,
                                 bias=nmr, scale=rstd).then_inc(sem["ac"], 1)
            cnt["ac"] += 2 * NT
            acA = ac0

            @block.gpsimd
            def _(e):
                for i in range(NT):
                    e.wait_ge(sem["ac"], acA + 2 * i + 2)
                    cnt["stG"] += 16
                    e.dma_start(out=xnb[128 * i:128 * (i + 1), :],
                                in_=sxn[i % 2][:]).then_inc(sem["stG"], 16)

            barrier(block)

        def mm_pass(tag, wdram, win_rows, nmch, src_is_tokmajor_dram, src_dram, dst_dram,
                    ngroups, ntiles_per_group, kch, loads_only=False):
            ntot = ntiles_per_group * 512
            with ExitStack() as stB:
                wb = [stB.enter_context(nc.sbuf_tensor(f"{tag}_w{k}", [128, nmch * 128], BF16))
                      for k in range(kch)]
                xg = [stB.enter_context(nc.sbuf_tensor(f"{tag}_x{k}", [128, ntot], BF16))
                      for k in range(kch)]
                ev = [stB.enter_context(nc.sbuf_tensor(f"{tag}_ev{i}", [128, 512], BF16))
                      for i in range(ntiles_per_group)]
                pb = [stB.enter_context(nc.psum_tensor(f"{tag}_p{i}", [128, 512], F32))
                      for i in range(ntiles_per_group)]

                ldG0, ldT0, ldS0, pe0, ac0, stS0 = (cnt["ldG"], cnt["ldT"], cnt["ldS"],
                                                    cnt["pe"], cnt["ac"], cnt["stS"])
                NTL = ntiles_per_group

                @block.gpsimd
                def _(e):
                    for k in range(kch):
                        cnt["ldG"] += 16
                        e.dma_start(out=wb[k][:], in_=wdram[128 * k:128 * (k + 1), :]).then_inc(sem["ldG"], 16)

                @block.sync
                def _(e):
                    for ng in range(ngroups):
                        if ng >= 1 and not loads_only:
                            e.wait_ge(sem["pe"], pe0 + nmch * ng)
                        for k in range(kch):
                            if src_is_tokmajor_dram:
                                cnt["ldT"] += 16
                                e.dma_start_transpose(
                                    xg[k][:], src_dram[ntot * ng:ntot * (ng + 1),
                                                       128 * k:128 * (k + 1)]).then_inc(sem["ldT"], 16)
                            else:
                                cnt["ldS"] += 16
                                e.dma_start(out=xg[k][:],
                                            in_=src_dram[128 * k:128 * (k + 1),
                                                         ntot * ng:ntot * (ng + 1)]).then_inc(sem["ldS"], 16)
                        for M in range(nmch):
                            if loads_only:
                                break
                            gm = ng * nmch + M
                            for nt in range(NTL):
                                e.wait_ge(sem["ac"], ac0 + gm * NTL + nt + 1)
                                cnt["stS"] += 16
                                e.dma_start(out=dst_dram[128 * M:128 * (M + 1),
                                                         ntot * ng + 512 * nt:ntot * ng + 512 * (nt + 1)],
                                            in_=ev[nt][:]).then_inc(sem["stS"], 16)

                @block.tensor
                def _(e):
                    if loads_only:
                        return
                    e.wait_ge(sem["ldG"], ldG0 + 16 * kch)
                    for ng in range(ngroups):
                        if src_is_tokmajor_dram:
                            e.wait_ge(sem["ldT"], ldT0 + 16 * kch * (ng + 1))
                        else:
                            e.wait_ge(sem["ldS"], ldS0 + 16 * kch * (ng + 1))
                        for M in range(nmch):
                            gm = ng * nmch + M
                            if gm >= 1:
                                e.wait_ge(sem["ac"], ac0 + gm * NTL)
                            for k in range(kch):
                                for nt in range(NTL):
                                    mm = e.matmul(pb[nt][:], wb[k][:, 128 * M:128 * (M + 1)],
                                                  xg[k][:, 512 * nt:512 * (nt + 1)],
                                                  start=(k == 0), stop=(k == kch - 1))
                            mm.then_inc(sem["pe"], 1)
                cnt["pe"] += ngroups * nmch

                @block.scalar
                def _(e):
                    if loads_only:
                        return
                    for ng in range(ngroups):
                        for M in range(nmch):
                            gm = ng * nmch + M
                            e.wait_ge(sem["pe"], pe0 + gm + 1)
                            if gm >= 1:
                                e.wait_ge(sem["stS"], stS0 + 16 * NTL * gm)
                            for nt in range(NTL):
                                e.activation(out=ev[nt][:], in_=pb[nt][:],
                                             func=AF.Copy).then_inc(sem["ac"], 1)
                cnt["ac"] += ngroups * nmch * NTL

                barrier(block)

        if "B" in phases:
            mm_pass("B", w1t, D, 26, True, xnb, qkvT, ngroups=4, ntiles_per_group=4, kch=8)
        if "b" in phases:
            mm_pass("B", w1t, D, 26, True, xnb, qkvT, ngroups=4, ntiles_per_group=4, kch=8,
                    loads_only=True)

        if "C" in phases:
         with ExitStack() as stC:
            sqt = [stC.enter_context(nc.sbuf_tensor(f"C_qt{b}", [128, 256], BF16)) for b in range(2)]
            sP = stC.enter_context(nc.sbuf_tensor("C_P", [128, 1024], F32))
            sP8 = stC.enter_context(nc.sbuf_tensor("C_P8", [128, 64], F32))
            st64 = stC.enter_context(nc.sbuf_tensor("C_t64", [128, 64], F32))
            sSq = stC.enter_context(nc.sbuf_tensor("C_Sq", [128, 64], F32))
            sIx = stC.enter_context(nc.sbuf_tensor("C_Ix", [128, 64], F32))
            sU = stC.enter_context(nc.sbuf_tensor("C_U", [128, 32], F32))
            sV = stC.enter_context(nc.sbuf_tensor("C_V", [128, 32], F32))
            sK = stC.enter_context(nc.sbuf_tensor("C_K", [128, 256], F32))
            sK8 = stC.enter_context(nc.sbuf_tensor("C_K8", [128, 32], F32))
            sKq = stC.enter_context(nc.sbuf_tensor("C_Kq", [128, 32], F32))
            sM8 = stC.enter_context(nc.sbuf_tensor("C_M8", [128, 32], F32))
            sI8 = stC.enter_context(nc.sbuf_tensor("C_I8", [128, 32], F32))
            sJ8 = stC.enter_context(nc.sbuf_tensor("C_J8", [128, 32], F32))
            sCmp = stC.enter_context(nc.sbuf_tensor("C_Cmp", [128, 256], F32))
            sSel = stC.enter_context(nc.sbuf_tensor("C_Sel", [128, 32], F32))
            sE = stC.enter_context(nc.sbuf_tensor("C_E", [128, 32], F32))
            sG1 = stC.enter_context(nc.sbuf_tensor("C_G1", [128, 32], F32))
            sGm = stC.enter_context(nc.sbuf_tensor("C_Gm", [128, 4], F32))
            sEi = stC.enter_context(nc.sbuf_tensor("C_Ei", [128, 32], I16))
            sEt = stC.enter_context(nc.sbuf_tensor("C_Et", [128, 32], I16))
            pC = stC.enter_context(nc.psum_tensor("C_ps", [128, 1024], F32))
            pG = stC.enter_context(nc.psum_tensor("C_pg", [128, 32], F32))

            ldS0, pe0, ve0, ac0, stG0 = cnt["ldS"], cnt["pe"], cnt["ve"], cnt["ac"], cnt["stG"]
            VP = 9

            @block.sync
            def _(e):
                for i in range(NT):
                    if i >= 2:
                        e.wait_ge(sem["pe"], pe0 + 2 * (i - 1))
                    for c in range(2):
                        cnt["ldS"] += 16
                        e.dma_start(out=sqt[i % 2][:, 128 * c:128 * (c + 1)],
                                    in_=qkvT[3072 + 128 * c:3072 + 128 * (c + 1),
                                             128 * i:128 * (i + 1)]).then_inc(sem["ldS"], 16)

            @block.tensor
            def _(e):
                for i in range(NT):
                    e.wait_ge(sem["ldS"], ldS0 + 32 * (i + 1))
                    if i >= 1:
                        e.wait_ge(sem["ac"], ac0 + 2 * (i - 1) + 1)
                    for c in range(2):
                        mm = e.matmul(pC[:, 512 * c:512 * (c + 1)],
                                      sqt[i % 2][:, 128 * c:128 * (c + 1)],
                                      skbd[:, 512 * c:512 * (c + 1)],
                                      start=True, stop=True)
                    mm.then_inc(sem["pe"], 1)
                    e.wait_ge(sem["ve"], ve0 + VP * i + 6)
                    if i >= 1:
                        e.wait_ge(sem["ve"], ve0 + VP * (i - 1) + 9)
                    e.matmul(pG[:], sPSIG[:], sG1[:], start=True, stop=True).then_inc(sem["pe"], 1)
            cnt["pe"] += 2 * NT

            @block.scalar
            def _(e):
                for i in range(NT):
                    e.wait_ge(sem["pe"], pe0 + 2 * i + 1)
                    if i >= 1:
                        e.wait_ge(sem["ve"], ve0 + VP * (i - 1) + 5)
                    e.activation(out=sP[:], in_=pC[:], func=AF.Copy,
                                 scale=float(QSCALE)).then_inc(sem["ac"], 1)
                    e.wait_ge(sem["ve"], ve0 + VP * i + 5)
                    e.activation(out=sK8[:], in_=sK8[:], func=AF.Exp).then_inc(sem["ac"], 1)
            cnt["ac"] += 2 * NT

            @block.vector
            def _(e):
                for i in range(NT):
                    vb = ve0 + VP * i
                    e.wait_ge(sem["ac"], ac0 + 2 * i + 1)
                    e.tensor_scalar(out=sP[:], in0=sP[:], scalar1=2047.0, scalar2=-2047.0,
                                    op0=ALU.min, op1=ALU.max)
                    e.tensor_scalar(out=sP[:], in0=sP[:], scalar1=MAGIC, scalar2=MAGIC,
                                    op0=ALU.add, op1=ALU.subtract)
                    e.scalar_tensor_tensor(out=sP[:], in0=sP[:], scalar=128.0,
                                           op0=ALU.mult, in1=sIOTAOFFC[:], op1=ALU.add).then_inc(sem["ve"], 1)
                    e.wait_ge(sem["ve"], vb + 1)
                    for g in range(8):
                        mm = e.max(out=sP8[:, 8 * g:8 * (g + 1)], in_=sP[:, 128 * g:128 * (g + 1)])
                    mm.then_inc(sem["ve"], 1)
                    e.wait_ge(sem["ve"], vb + 2)
                    e.tensor_scalar(out=st64[:], in0=sP8[:], scalar1=1.0 / 128, scalar2=63.5 / 128,
                                    op0=ALU.mult, op1=ALU.subtract).then_inc(sem["veg"], 1)
                    e.wait_ge(sem["veg"], vegadv())
                    e.tensor_scalar(out=sSq[:], in0=st64[:], scalar1=MAGIC, scalar2=MAGIC,
                                    op0=ALU.add, op1=ALU.subtract).then_inc(sem["veg"], 1)
                    e.wait_ge(sem["veg"], vegadv())
                    e.scalar_tensor_tensor(out=sIx[:], in0=sSq[:], scalar=-128.0,
                                           op0=ALU.mult, in1=sP8[:], op1=ALU.add).then_inc(sem["veg"], 1)
                    e.wait_ge(sem["veg"], vegadv())
                    e.tensor_scalar(out=sIx[:], in0=sIx[:], scalar1=-1.0, scalar2=127.0,
                                    op0=ALU.mult, op1=ALU.add).then_inc(sem["veg"], 1)
                    e.wait_ge(sem["veg"], vegadv())
                    sqv = sSq.rearrange("q (h p r) -> q h p r", h=4, p=2)
                    e.scalar_tensor_tensor(out=sU.rearrange("q (h r) -> q h r", h=4),
                                           in0=sqv[:, :, 0, :], scalar=64.0, op0=ALU.mult,
                                           in1=sIOTAI4.rearrange("q (h r) -> q h r", h=4),
                                           op1=ALU.add).then_inc(sem["veg"], 1)
                    e.wait_ge(sem["veg"], vegadv())
                    e.scalar_tensor_tensor(out=sV.rearrange("q (h r) -> q h r", h=4),
                                           in0=sqv[:, :, 1, :], scalar=64.0, op0=ALU.mult,
                                           in1=sIOTAJ4.rearrange("q (h r) -> q h r", h=4),
                                           op1=ALU.add).then_inc(sem["veg"], 1)
                    e.wait_ge(sem["veg"], vegadv())
                    e.tensor_tensor(out=sK.rearrange("q (h i j) -> q h i j", h=4, i=8),
                                    in0=sU.rearrange("q (h i) -> q h i", h=4).unsqueeze(3).broadcast_to([128, 4, 8, 8]),
                                    in1=sV.rearrange("q (h j) -> q h j", h=4).unsqueeze(2).broadcast_to([128, 4, 8, 8]),
                                    op=ALU.add).then_inc(sem["ve"], 1)
                    e.wait_ge(sem["ve"], vb + 3)
                    for h in range(4):
                        mm = e.max(out=sK8[:, 8 * h:8 * (h + 1)], in_=sK[:, 64 * h:64 * (h + 1)])
                    mm.then_inc(sem["ve"], 1)
                    e.wait_ge(sem["ve"], vb + 4)
                    e.tensor_scalar(out=sKq[:], in0=sK8[:], scalar1=1.0 / 64, scalar2=31.5 / 64,
                                    op0=ALU.mult, op1=ALU.subtract).then_inc(sem["veg"], 1)
                    e.wait_ge(sem["veg"], vegadv())
                    e.tensor_scalar(out=sKq[:], in0=sKq[:], scalar1=MAGIC, scalar2=MAGIC,
                                    op0=ALU.add, op1=ALU.subtract).then_inc(sem["veg"], 1)
                    e.wait_ge(sem["veg"], vegadv())
                    e.scalar_tensor_tensor(out=sM8[:], in0=sKq[:], scalar=-64.0,
                                           op0=ALU.mult, in1=sK8[:], op1=ALU.add).then_inc(sem["veg"], 1)
                    e.wait_ge(sem["veg"], vegadv())
                    e.tensor_scalar(out=sM8[:], in0=sM8[:], scalar1=-1.0, scalar2=63.0,
                                    op0=ALU.mult, op1=ALU.add).then_inc(sem["veg"], 1)
                    e.wait_ge(sem["veg"], vegadv())
                    e.tensor_scalar(out=sI8[:], in0=sM8[:], scalar1=1.0 / 8, scalar2=3.5 / 8,
                                    op0=ALU.mult, op1=ALU.subtract).then_inc(sem["veg"], 1)
                    e.wait_ge(sem["veg"], vegadv())
                    e.tensor_scalar(out=sI8[:], in0=sI8[:], scalar1=MAGIC, scalar2=MAGIC,
                                    op0=ALU.add, op1=ALU.subtract).then_inc(sem["veg"], 1)
                    e.wait_ge(sem["veg"], vegadv())
                    e.scalar_tensor_tensor(out=sJ8[:], in0=sI8[:], scalar=-8.0,
                                           op0=ALU.mult, in1=sM8[:], op1=ALU.add).then_inc(sem["veg"], 1)
                    e.wait_ge(sem["veg"], vegadv())
                    ixv = sIx.rearrange("q (h p r) -> q h p r", h=4, p=2)
                    for half, selv in ((0, sI8), (1, sJ8)):
                        e.tensor_tensor(out=sCmp.rearrange("q (h k r) -> q h k r", h=4, k=8),
                                        in0=selv.rearrange("q (h k) -> q h k", h=4).unsqueeze(3).broadcast_to([128, 4, 8, 8]),
                                        in1=sIOTA8C[:].unsqueeze(1).unsqueeze(1).broadcast_to([128, 4, 8, 8]),
                                        op=ALU.is_equal).then_inc(sem["veg"], 1)
                        e.wait_ge(sem["veg"], vegadv())
                        e.tensor_tensor(out=sCmp.rearrange("q (h k r) -> q h k r", h=4, k=8),
                                        in0=sCmp.rearrange("q (h k r) -> q h k r", h=4, k=8),
                                        in1=ixv[:, :, half, :].unsqueeze(2).broadcast_to([128, 4, 8, 8]),
                                        op=ALU.mult).then_inc(sem["veg"], 1)
                        e.wait_ge(sem["veg"], vegadv())
                        e.tensor_reduce(out=sSel[:].unsqueeze(2),
                                        in_=sCmp.rearrange("q (g r) -> q g r", g=32),
                                        axis=AX.X, op=ALU.add).then_inc(sem["veg"], 1)
                        e.wait_ge(sem["veg"], vegadv())
                        if half == 0:
                            e.tensor_scalar_mul(sE[:], sSel[:], 128.0)
                        else:
                            e.tensor_add(sE[:], sE[:], sSel[:])
                    e.reduce_max(sGm[:].unsqueeze(2), sKq.rearrange("q (h k) -> q h k", h=4),
                                 axis=AX.X).then_inc(sem["veg"], 1)
                    e.wait_ge(sem["veg"], vegadv())
                    e.tensor_tensor(out=sK8.rearrange("q (h k) -> q h k", h=4),
                                    in0=sKq.rearrange("q (h k) -> q h k", h=4),
                                    in1=sGm[:].unsqueeze(2).broadcast_to([128, 4, 8]),
                                    op=ALU.subtract).then_inc(sem["veg"], 1)
                    e.wait_ge(sem["veg"], vegadv())
                    e.tensor_scalar(out=sK8[:], in0=sK8[:], scalar1=1.0 / QSCALE, scalar2=None,
                                    op0=ALU.mult).then_inc(sem["ve"], 1)
                    e.wait_ge(sem["ac"], ac0 + 2 * i + 2)
                    e.reduce_sum(sGm[:].unsqueeze(2), sK8.rearrange("q (h k) -> q h k", h=4),
                                 axis=AX.X).then_inc(sem["veg"], 1)
                    e.wait_ge(sem["veg"], vegadv())
                    e.reciprocal(sGm[:], sGm[:]).then_inc(sem["veg"], 1)
                    e.wait_ge(sem["veg"], vegadv())
                    e.tensor_tensor(out=sG1.rearrange("q (h k) -> q h k", h=4),
                                    in0=sK8.rearrange("q (h k) -> q h k", h=4),
                                    in1=sGm[:].unsqueeze(2).broadcast_to([128, 4, 8]),
                                    op=ALU.mult).then_inc(sem["ve"], 1)
                    e.tensor_copy(sEi[:], sE[:]).then_inc(sem["ve"], 1)
                    e.wait_ge(sem["ve"], vb + 7)
                    if i >= 1:
                        e.wait_ge(sem["stG"], stG0 + 16 * i)
                    e.transpose(out=sEt[:], in_=sEi[:]).then_inc(sem["ve"], 1)
                    e.wait_ge(sem["pe"], pe0 + 2 * i + 2)
                    e.tensor_copy(sgsig[:, 32 * i:32 * (i + 1)], pG[:]).then_inc(sem["ve"], 1)
            cnt["ve"] += VP * NT

            @block.gpsimd
            def _(e):
                for i in range(NT):
                    e.wait_ge(sem["ve"], ve0 + VP * i + 8)
                    cnt["stG"] += 16
                    e.dma_start(out=idx_sc[i], in_=sEt[:]).then_inc(sem["stG"], 16)
                    if "cdbg" in debug:
                        for nm, t in [("P8", sP8), ("Ix", sIx), ("Sq", sSq), ("K8", sK8),
                                      ("Kq", sKq), ("M8", sM8), ("I8", sI8), ("J8", sJ8),
                                      ("E", sE), ("G1", sG1)]:
                            cnt["stG"] += 16
                            e.dma_start(out=cdbg[nm][i], in_=t[:]).then_inc(sem["stG"], 16)

            barrier(block)

        if "D" in phases:
         with ExitStack() as stD:
            sqh = stD.enter_context(nc.sbuf_tensor("D_qh", [64, T], BF16))
            skh = stD.enter_context(nc.sbuf_tensor("D_kh", [64, T], BF16))
            svt = stD.enter_context(nc.sbuf_tensor("D_vt", [128, 4096], BF16))
            sS = [stD.enter_context(nc.sbuf_tensor(f"D_S{wh}", [128, 256], F32)) for wh in range(2)]
            sA = [stD.enter_context(nc.sbuf_tensor(f"D_A{wh}", [128, 256], BF16)) for wh in range(2)]
            sAT = [stD.enter_context(nc.sbuf_tensor(f"D_AT{xh}", [128, 256], BF16)) for xh in range(2)]
            sOT = stD.enter_context(nc.sbuf_tensor("D_OT", [64, T], BF16))
            sms = stD.enter_context(nc.sbuf_tensor("D_ms", [128, 4], F32))
            pS = [stD.enter_context(nc.psum_tensor(f"D_pS{wh}", [128, 256], F32)) for wh in range(2)]
            pAT = stD.enter_context(nc.psum_tensor("D_pAT", [128, 128], BF16))
            pO = [stD.enter_context(nc.psum_tensor(f"D_pO{pr}", [64, 512], F32)) for pr in range(2)]

            ldS0, ldT0, pe0, ve0, ac0, stS0 = (cnt["ldS"], cnt["ldT"], cnt["pe"],
                                               cnt["ve"], cnt["ac"], cnt["stS"])
            PEH = 22
            VEH = 8
            ACH = 20

            @block.sync
            def _(e):
                for h in range(HEADS):
                    if h >= 1:
                        e.wait_ge(sem["pe"], pe0 + PEH * h)
                        e.wait_ge(sem["ac"], ac0 + ACH * h)
                        cnt["stS"] += 16
                        e.dma_start(out=aT[64 * (h - 1):64 * h, :], in_=sOT[:]).then_inc(sem["stS"], 16)
                        e.wait_ge(sem["stS"], cnt["stS"])
                    cnt["ldS"] += 16
                    e.dma_start(out=sqh[:], in_=qkvT[64 * h:64 * (h + 1), :]).then_inc(sem["ldS"], 16)
                    cnt["ldS"] += 16
                    e.dma_start(out=skh[:], in_=qkvT[1024 + 64 * h:1024 + 64 * (h + 1), :]).then_inc(sem["ldS"], 16)
                    for b in range(64):
                        cnt["ldT"] += 16
                        e.dma_start_transpose(svt[:, 64 * b:64 * (b + 1)],
                                              qkvT[2048 + 64 * h:2048 + 64 * (h + 1),
                                                   128 * b:128 * (b + 1)]).then_inc(sem["ldT"], 16)
                e.wait_ge(sem["ac"], ac0 + ACH * HEADS)
                cnt["stS"] += 16
                e.dma_start(out=aT[64 * (HEADS - 1):64 * HEADS, :], in_=sOT[:]).then_inc(sem["stS"], 16)

            @block.tensor
            def _(e):
                for h in range(HEADS):
                    e.wait_ge(sem["ldS"], ldS0 + 32 * (h + 1))
                    for wh in range(2):
                        if h >= 1:
                            e.wait_ge(sem["ac"], ac0 + ACH * (h - 1) + 2)
                        for j in range(32):
                            mm = e.matmul(pS[wh][:],
                                          sqh[:, 256 * j + 128 * wh:256 * j + 128 * (wh + 1)],
                                          skh[:, 256 * j:256 * (j + 1)],
                                          start=(j == 0), stop=(j == 31))
                        mm.then_inc(sem["pe"], 1)
                    for t, (xh, wh) in enumerate([(0, 0), (0, 1), (1, 0), (1, 1)]):
                        e.wait_ge(sem["ve"], ve0 + VEH * h + 3 + wh)
                        if t >= 1:
                            e.wait_ge(sem["ve"], ve0 + VEH * h + 4 + t)
                        e.transpose(pAT[:], sA[wh][:, 128 * xh:128 * (xh + 1)], sIDENT[:]).then_inc(sem["pe"], 1)
                    e.wait_ge(sem["ldT"], ldT0 + 16 * 64 * (h + 1))
                    e.wait_ge(sem["ve"], ve0 + VEH * h + 8)
                    for jp in range(16):
                        pr = jp % 2
                        if jp >= 2:
                            e.wait_ge(sem["ac"], ac0 + ACH * h + 4 + jp - 1)
                        for jj in range(2):
                            j = 2 * jp + jj
                            for xh in range(2):
                                mm = e.matmul(pO[pr][:, 256 * jj:256 * (jj + 1)],
                                              svt[:, 64 * (2 * j + xh):64 * (2 * j + xh + 1)],
                                              sAT[xh][:],
                                              start=(xh == 0), stop=(xh == 1))
                        mm.then_inc(sem["pe"], 1)
            cnt["pe"] += PEH * HEADS

            @block.vector
            def _(e):
                for h in range(HEADS):
                    if h >= 1:
                        e.wait_ge(sem["pe"], pe0 + PEH * (h - 1) + 6)
                    for wh in range(2):
                        e.wait_ge(sem["ac"], ac0 + ACH * h + wh + 1)
                        e.reduce_max(sms[:, wh:wh + 1], sS[wh][:], axis=AX.X).then_inc(sem["veg"], 1)
                        e.wait_ge(sem["veg"], vegadv())
                        e.tensor_scalar(out=sS[wh][:], in0=sS[wh][:], scalar1=sms[:, wh:wh + 1],
                                        scalar2=None, op0=ALU.subtract).then_inc(sem["ve"], 1)
                    for wh in range(2):
                        e.wait_ge(sem["ac"], ac0 + ACH * h + 2 + wh + 1)
                        e.reduce_sum(sms[:, 2 + wh:3 + wh], sS[wh][:], axis=AX.X).then_inc(sem["veg"], 1)
                        e.wait_ge(sem["veg"], vegadv())
                        e.reciprocal(sms[:, 2 + wh:3 + wh], sms[:, 2 + wh:3 + wh]).then_inc(sem["veg"], 1)
                        e.wait_ge(sem["veg"], vegadv())
                        e.tensor_scalar(out=sA[wh][:], in0=sS[wh][:],
                                        scalar1=sms[:, 2 + wh:3 + wh], scalar2=None,
                                        op0=ALU.mult).then_inc(sem["ve"], 1)
                    for xh in range(2):
                        for wh in range(2):
                            e.wait_ge(sem["pe"], pe0 + PEH * h + 2 + 2 * xh + wh + 1)
                            e.tensor_copy(sAT[xh][:, 128 * wh:128 * (wh + 1)], pAT[:]).then_inc(sem["ve"], 1)
            cnt["ve"] += VEH * HEADS

            @block.scalar
            def _(e):
                for h in range(HEADS):
                    if h >= 1:
                        e.wait_ge(sem["ve"], ve0 + VEH * (h - 1) + 4)
                        e.wait_ge(sem["stS"], stS0 + 16 * h)
                    for wh in range(2):
                        e.wait_ge(sem["pe"], pe0 + PEH * h + wh + 1)
                        e.activation(out=sS[wh][:], in_=pS[wh][:], func=AF.Copy,
                                     scale=0.125).then_inc(sem["ac"], 1)
                    for wh in range(2):
                        e.wait_ge(sem["ve"], ve0 + VEH * h + wh + 1)
                        e.activation(out=sS[wh][:], in_=sS[wh][:],
                                     func=AF.Exp).then_inc(sem["ac"], 1)
                    for jp in range(16):
                        e.wait_ge(sem["pe"], pe0 + PEH * h + 6 + jp + 1)
                        e.activation(out=sOT[:, 512 * jp:512 * (jp + 1)], in_=pO[jp % 2][:],
                                     func=AF.Copy).then_inc(sem["ac"], 1)
            cnt["ac"] += ACH * HEADS

            barrier(block)

        if "P" in phases:
            mm_pass("P", wpt, D, 8, False, aT, sT, ngroups=4, ntiles_per_group=4, kch=8)

        if "E" in phases:
         with ExitStack() as stE:
            sxnT = [stE.enter_context(nc.sbuf_tensor(f"E_xt{b}", [128, 8 * 128], BF16)) for b in range(2)]
            sxs = stE.enter_context(nc.sbuf_tensor("E_xs", [128, 8 * 128], BF16))
            sidx = [stE.enter_context(nc.sbuf_tensor(f"E_ix{b}", [128, 256], I16)) for b in range(2)]
            sdeT = [stE.enter_context(nc.sbuf_tensor(f"E_de{b}", [128, 8 * 512], BF16)) for b in range(2)]
            sue = [stE.enter_context(nc.sbuf_tensor(f"E_ue{b}", [128, 4 * 1024], BF16)) for b in range(2)]
            sMask = stE.enter_context(nc.sbuf_tensor("E_msk", [128, 512], BF16))
            sewr = stE.enter_context(nc.sbuf_tensor("E_ewr", [128, 4], F32))
            sews = stE.enter_context(nc.sbuf_tensor("E_ews", [128, 4], F32))
            sewf = stE.enter_context(nc.sbuf_tensor("E_ewf", [128, 4], F32))
            sGt = [stE.enter_context(nc.sbuf_tensor(f"E_G{b}", [128, 128], BF16)) for b in range(4)]
            sesb = [stE.enter_context(nc.sbuf_tensor(f"E_es{b}", [128, 1024], BF16)) for b in range(2)]
            pD = [stE.enter_context(nc.psum_tensor(f"E_pD{b}", [128, 512], F32)) for b in range(2)]
            pES = stE.enter_context(nc.psum_tensor("E_pES", [128, 1024], F32))

            ldS0, ldT0, pe0, ve0, ac0, stG0, gat0 = (cnt["ldS"], cnt["ldT"], cnt["pe"],
                                                     cnt["ve"], cnt["ac"], cnt["stG"], cnt["gat"])
            PET, VET, ACT_, GATT = 16, 17, 9, 8 * 32

            @block.sync
            def _(e):
                for i in range(NT):
                    if i >= 2:
                        e.wait_ge(sem["ve"], ve0 + VET * (i - 1) + 1)
                    for k in range(8):
                        cnt["ldT"] += 16
                        e.dma_start_transpose(sxnT[i % 2][:, 128 * k:128 * (k + 1)],
                                              xnb[128 * i:128 * (i + 1),
                                                  128 * k:128 * (k + 1)]).then_inc(sem["ldT"], 16)
                    if i >= 2:
                        e.wait_ge(sem["gat"], gat0 + GATT * (i - 1))
                    srcv = idx_sc[i].rearrange("(r j th tl) -> r th j tl", r=4, j=32, th=4)
                    for half in range(2):
                        for rr in range(4):
                            dst = sidx[i % 2][16 * half + 4 * rr:16 * half + 4 * rr + 4, :]
                            cnt["ldS"] += 16
                            e.dma_start(out=dst.rearrange("th (j tl) -> th j tl", tl=8),
                                        in_=srcv[rr]).then_inc(sem["ldS"], 16)

            @block.gpsimd
            def _(e):
                for i in range(NT):
                    e.wait_ge(sem["ldS"], ldS0 + 16 * 8 * (i + 1))
                    for g in range(8):
                        gg = 8 * i + g
                        if gg >= 2:
                            e.wait_ge(sem["pe"], pe0 + PET * ((gg - 2) // 8) + 2 * ((gg - 2) % 8) + 2)
                        b = gg % 2
                        cnt["gat"] += 16
                        e.dma_gather(sdeT[b][:].rearrange("p (c n) -> p c n", c=8),
                                     de_bf[:], sidx[i % 2][:, 32 * g:32 * (g + 1)],
                                     512, 512, D, transpose=True).then_inc(sem["gat"], 16)
                        cnt["gat"] += 16
                        e.dma_gather(sue[b][:].rearrange("p (c n) -> p c n", c=4),
                                     ue_bf[:], sidx[i % 2][:, 32 * g:32 * (g + 1)],
                                     512, 512, D).then_inc(sem["gat"], 16)
                    if i >= 1:
                        e.wait_ge(sem["ac"], ac0 + ACT_ * i)
                        cnt["stG"] += 16
                        e.dma_start(out=es_sc[128 * (i - 1):128 * i, :],
                                    in_=sesb[(i - 1) % 2][:]).then_inc(sem["stG"], 16)
                e.wait_ge(sem["ac"], ac0 + ACT_ * NT)
                cnt["stG"] += 16
                e.dma_start(out=es_sc[128 * (NT - 1):128 * NT, :],
                            in_=sesb[(NT - 1) % 2][:]).then_inc(sem["stG"], 16)

            @block.vector
            def _(e):
                for i in range(NT):
                    vb = ve0 + VET * i
                    e.wait_ge(sem["ldT"], ldT0 + 16 * 8 * (i + 1))
                    if i >= 1:
                        e.wait_ge(sem["pe"], pe0 + PET * (i - 1) + 15)
                    for k in range(8):
                        mm = e.tensor_copy(sxs[:, 128 * k:128 * (k + 1)].rearrange("p (o ii) -> p o ii", o=8),
                                           sxnT[i % 2][:, 128 * k:128 * (k + 1)].rearrange("p (ii o) -> p o ii", ii=16))
                    mm.then_inc(sem["ve"], 1)
                    for g in range(8):
                        e.wait_ge(sem["pe"], pe0 + PET * i + 2 * g + 1)
                        if g >= 1:
                            e.wait_ge(sem["ac"], ac0 + ACT_ * i + g)
                        elif i >= 1:
                            e.wait_ge(sem["ac"], ac0 + ACT_ * (i - 1) + 8)
                        e.tensor_tensor(out=sMask[:], in0=pD[g % 2][:], in1=sMDIAG[:],
                                        op=ALU.mult).then_inc(sem["veg"], 1)
                        e.wait_ge(sem["veg"], vegadv())
                        e.tensor_reduce(out=sewr[:].unsqueeze(2),
                                        in_=sMask.rearrange("p (c r) -> p c r", c=4),
                                        axis=AX.X, op=ALU.add).then_inc(sem["ve"], 1)
                        e.wait_ge(sem["ac"], ac0 + ACT_ * i + g + 1)
                        e.tensor_tensor(out=sewf[:], in0=sews[:],
                                        in1=sgsig[:, 32 * i + 4 * g:32 * i + 4 * (g + 1)],
                                        op=ALU.mult).then_inc(sem["veg"], 1)
                        e.wait_ge(sem["veg"], vegadv())
                        if g >= 1:
                            e.wait_ge(sem["pe"], pe0 + PET * i + 2 * g)
                        elif i >= 1:
                            e.wait_ge(sem["pe"], pe0 + PET * i)
                        for jl in range(4):
                            mm = e.tensor_scalar(out=sGt[jl][:], in0=sESIG[:],
                                                 scalar1=sewf[:, jl:jl + 1], scalar2=None,
                                                 op0=ALU.mult)
                        mm.then_inc(sem["ve"], 1)
            cnt["ve"] += VET * NT

            @block.tensor
            def _(e):
                for i in range(NT):
                    e.wait_ge(sem["ve"], ve0 + VET * i + 1)
                    for g in range(8):
                        gg = 8 * i + g
                        b = gg % 2
                        e.wait_ge(sem["gat"], gat0 + 32 * gg + 32)
                        if g >= 2:
                            e.wait_ge(sem["ve"], ve0 + VET * i + 2 + 2 * (g - 2))
                        elif i >= 1 and g < 2:
                            e.wait_ge(sem["ve"], ve0 + VET * (i - 1) + 2 + 2 * (6 + g))
                        for k in range(8):
                            mm = e.matmul(pD[b][:], sxs[:, 128 * k:128 * (k + 1)],
                                          sdeT[b][:, 512 * k:512 * (k + 1)],
                                          start=(k == 0), stop=(k == 7))
                        mm.then_inc(sem["pe"], 1)
                        e.wait_ge(sem["ve"], ve0 + VET * i + 3 + 2 * g)
                        if i >= 1 and g == 0:
                            e.wait_ge(sem["ac"], ac0 + ACT_ * i)
                        for jl in range(4):
                            for hf in range(2):
                                mm = e.matmul(pES[:, 512 * hf:512 * (hf + 1)],
                                              sGt[jl][:],
                                              sue[b][:, 1024 * jl + 512 * hf:1024 * jl + 512 * (hf + 1)],
                                              start=(g == 0 and jl == 0), stop=(g == 7 and jl == 3))
                        mm.then_inc(sem["pe"], 1)
            cnt["pe"] += PET * NT

            @block.scalar
            def _(e):
                for i in range(NT):
                    for g in range(8):
                        e.wait_ge(sem["ve"], ve0 + VET * i + 2 + 2 * g)
                        e.activation(out=sews[:], in_=sewr[:], func=AF.Silu).then_inc(sem["ac"], 1)
                    e.wait_ge(sem["pe"], pe0 + PET * i + 16)
                    if i >= 2:
                        e.wait_ge(sem["stG"], stG0 + 16 * (i - 1))
                    e.activation(out=sesb[i % 2][:], in_=pES[:], func=AF.Copy).then_inc(sem["ac"], 1)
            cnt["ac"] += ACT_ * NT

            barrier(block)

        if "F" in phases:
         with ExitStack() as stF:
            wgu = [stF.enter_context(nc.sbuf_tensor(f"F_wg{k}", [128, 4096], BF16)) for k in range(8)]
            wdb = [stF.enter_context(nc.sbuf_tensor(f"F_wd{m}", [128, 1024], BF16)) for m in range(16)]
            xg2 = [stF.enter_context(nc.sbuf_tensor(f"F_x{k}", [128, 1024], BF16)) for k in range(8)]
            sgu = [stF.enter_context(nc.sbuf_tensor(f"F_gu{i}", [128, 1024], BF16)) for i in range(2)]
            itg = [stF.enter_context(nc.sbuf_tensor(f"F_it{m}", [128, 1024], BF16)) for m in range(16)]
            smo = [stF.enter_context(nc.sbuf_tensor(f"F_mo{i}", [128, 512], BF16)) for i in range(4)]
            pGU = [stF.enter_context(nc.psum_tensor(f"F_pgu{i}", [128, 1024], F32)) for i in range(2)]
            pDn = [stF.enter_context(nc.psum_tensor(f"F_pd{i}", [128, 512], F32)) for i in range(2)]

            ldG0, ldT0, pe0, ve0, ac0, stS0 = (cnt["ldG"], cnt["ldT"], cnt["pe"],
                                               cnt["ve"], cnt["ac"], cnt["stS"])
            PEF, ACF, VEF = 24, 48, 16

            @block.gpsimd
            def _(e):
                for k in range(8):
                    cnt["ldG"] += 16
                    e.dma_start(out=wgu[k][:], in_=wgut[128 * k:128 * (k + 1), :]).then_inc(sem["ldG"], 16)
                for m in range(16):
                    cnt["ldG"] += 16
                    e.dma_start(out=wdb[m][:], in_=wdt[128 * m:128 * (m + 1), :]).then_inc(sem["ldG"], 16)

            @block.sync
            def _(e):
                for ng in range(8):
                    if ng >= 1:
                        e.wait_ge(sem["pe"], pe0 + PEF * ng)
                    for k in range(8):
                        cnt["ldT"] += 16
                        e.dma_start_transpose(xg2[k][:],
                                              xnb[1024 * ng:1024 * (ng + 1),
                                                  128 * k:128 * (k + 1)]).then_inc(sem["ldT"], 16)
                    for dout in range(8):
                        for nt in range(2):
                            e.wait_ge(sem["ac"], ac0 + ACF * ng + 32 + 2 * dout + nt + 1)
                            cnt["stS"] += 16
                            e.dma_start(out=mlpT[128 * dout:128 * (dout + 1),
                                                 1024 * ng + 512 * nt:1024 * ng + 512 * (nt + 1)],
                                        in_=smo[2 * (dout % 2) + nt][:]).then_inc(sem["stS"], 16)

            @block.tensor
            def _(e):
                e.wait_ge(sem["ldG"], ldG0 + 16 * 24)
                for ng in range(8):
                    e.wait_ge(sem["ldT"], ldT0 + 16 * 8 * (ng + 1))
                    for m in range(16):
                        if m >= 1:
                            e.wait_ge(sem["ac"], ac0 + ACF * ng + 2 * m)
                        elif ng >= 1:
                            e.wait_ge(sem["ac"], ac0 + ACF * (ng - 1) + 32)
                        for k in range(8):
                            for nt in range(2):
                                e.matmul(pGU[0][:, 512 * nt:512 * (nt + 1)],
                                         wgu[k][:, 128 * m:128 * (m + 1)],
                                         xg2[k][:, 512 * nt:512 * (nt + 1)],
                                         start=(k == 0), stop=(k == 7))
                                mm = e.matmul(pGU[1][:, 512 * nt:512 * (nt + 1)],
                                              wgu[k][:, 2048 + 128 * m:2048 + 128 * (m + 1)],
                                              xg2[k][:, 512 * nt:512 * (nt + 1)],
                                              start=(k == 0), stop=(k == 7))
                        mm.then_inc(sem["pe"], 1)
                    e.wait_ge(sem["ve"], ve0 + VEF * (ng + 1))
                    for dout in range(8):
                        if dout >= 1:
                            e.wait_ge(sem["ac"], ac0 + ACF * ng + 32 + 2 * dout)
                        elif ng >= 1:
                            e.wait_ge(sem["ac"], ac0 + ACF * ng)
                        for nt in range(2):
                            for m in range(16):
                                mm = e.matmul(pDn[nt][:], wdb[m][:, 128 * dout:128 * (dout + 1)],
                                              itg[m][:, 512 * nt:512 * (nt + 1)],
                                              start=(m == 0), stop=(m == 15))
                        mm.then_inc(sem["pe"], 1)
            cnt["pe"] += PEF * 8

            @block.scalar
            def _(e):
                for ng in range(8):
                    for m in range(16):
                        e.wait_ge(sem["pe"], pe0 + PEF * ng + m + 1)
                        if m >= 1:
                            e.wait_ge(sem["ve"], ve0 + VEF * ng + m)
                        elif ng >= 1:
                            e.wait_ge(sem["ve"], ve0 + VEF * ng)
                        e.activation(out=sgu[0][:], in_=pGU[0][:], func=AF.Silu).then_inc(sem["ac"], 1)
                        e.activation(out=sgu[1][:], in_=pGU[1][:], func=AF.Copy).then_inc(sem["ac"], 1)
                    for dout in range(8):
                        e.wait_ge(sem["pe"], pe0 + PEF * ng + 16 + dout + 1)
                        if ng * 8 + dout >= 2:
                            pg = ng * 8 + dout - 2
                            e.wait_ge(sem["stS"], stS0 + 16 * (2 * pg + 2))
                        for nt in range(2):
                            e.activation(out=smo[2 * (dout % 2) + nt][:], in_=pDn[nt][:],
                                         func=AF.Copy).then_inc(sem["ac"], 1)
            cnt["ac"] += ACF * 8

            @block.vector
            def _(e):
                for ng in range(8):
                    if ng >= 1:
                        e.wait_ge(sem["pe"], pe0 + PEF * ng)
                    for m in range(16):
                        e.wait_ge(sem["ac"], ac0 + ACF * ng + 2 * m + 2)
                        e.tensor_tensor(out=itg[m][:], in0=sgu[0][:], in1=sgu[1][:],
                                        op=ALU.mult).then_inc(sem["ve"], 1)
            cnt["ve"] += VEF * 8

            barrier(block)

        if "G" in phases:
         with ExitStack() as stG_:
            gx = [stG_.enter_context(nc.sbuf_tensor(f"G_x{b}", [128, D], F32)) for b in range(2)]
            gs = [stG_.enter_context(nc.sbuf_tensor(f"G_s{b}", [128, D], BF16)) for b in range(2)]
            gm = [stG_.enter_context(nc.sbuf_tensor(f"G_m{b}", [128, D], BF16)) for b in range(2)]
            ge = [stG_.enter_context(nc.sbuf_tensor(f"G_e{b}", [128, D], BF16)) for b in range(2)]
            gf = stG_.enter_context(nc.sbuf_tensor("G_f", [128, D], F32))
            gg = stG_.enter_context(nc.sbuf_tensor("G_g", [128, D], BF16))
            gt = stG_.enter_context(nc.sbuf_tensor("G_t", [128, D], F32))
            go = [stG_.enter_context(nc.sbuf_tensor(f"G_o{b}", [128, D], F32)) for b in range(2)]

            ldS0, ldT0, ve0, ac0, stS0 = (cnt["ldS"], cnt["ldT"], cnt["ve"], cnt["ac"], cnt["stS"])

            @block.sync
            def _(e):
                for i in range(NT):
                    if i >= 2:
                        e.wait_ge(sem["stS"], stS0 + 16 * (i - 1))
                    cnt["ldS"] += 16
                    e.dma_start(out=gx[i % 2][:], in_=x[128 * i:128 * (i + 1), :]).then_inc(sem["ldS"], 16)
                    cnt["ldS"] += 16
                    e.dma_start(out=ge[i % 2][:], in_=es_sc[128 * i:128 * (i + 1), :]).then_inc(sem["ldS"], 16)
                    cnt["ldT"] += 16
                    e.dma_start_transpose(gs[i % 2][:], sT[:, 128 * i:128 * (i + 1)]).then_inc(sem["ldT"], 16)
                    cnt["ldT"] += 16
                    e.dma_start_transpose(gm[i % 2][:], mlpT[:, 128 * i:128 * (i + 1)]).then_inc(sem["ldT"], 16)
                    e.wait_ge(sem["ve"], ve0 + 2 * (i + 1))
                    cnt["stS"] += 16
                    e.dma_start(out=out[128 * i:128 * (i + 1), :], in_=go[i % 2][:]).then_inc(sem["stS"], 16)

            @block.vector
            def _(e):
                for i in range(NT):
                    e.wait_ge(sem["ldS"], ldS0 + 32 * (i + 1))
                    e.wait_ge(sem["ldT"], ldT0 + 32 * (i + 1))
                    e.tensor_add(gf[:], gm[i % 2][:], ge[i % 2][:]).then_inc(sem["ve"], 1)
                    e.wait_ge(sem["ac"], ac0 + i + 1)
                    e.tensor_tensor(out=gt[:], in0=gg[:], in1=gs[i % 2][:],
                                    op=ALU.mult).then_inc(sem["veg"], 1)
                    e.wait_ge(sem["veg"], vegadv())
                    e.tensor_add(go[i % 2][:], gt[:], gx[i % 2][:]).then_inc(sem["ve"], 1)
            cnt["ve"] += 2 * NT

            @block.scalar
            def _(e):
                for i in range(NT):
                    e.wait_ge(sem["ve"], ve0 + 2 * i + 1)
                    e.activation(out=gg[:], in_=gf[:], func=AF.Silu).then_inc(sem["ac"], 1)
            cnt["ac"] += NT

            barrier(block)

    st.close()
    nc.compile()
    return nc


def _prep_inputs(inputs):
    x = inputs["L0_attn_x"]
    Wqkv = inputs["L0_attn_Wqkv"]
    Wproj = inputs["L0_attn_Wproj"]
    wq = inputs["L0_ffn_w_queries"]
    keys = inputs["L0_ffn_keys"]
    de = inputs["L0_ffn_down_embed"]
    ue = inputs["L0_ffn_up_embed"]
    wg = inputs["L0_ffn_w_gate"]
    wu = inputs["L0_ffn_w_up"]
    wd = inputs["L0_ffn_w_down"]

    wq_r = np.ascontiguousarray(
        wq.reshape(2, 4, 32, D).transpose(1, 0, 2, 3).reshape(256, D))
    w1t = np.ascontiguousarray(np.concatenate([Wqkv, wq_r], axis=0).T)
    wpt = np.ascontiguousarray(Wproj.T)
    wgut = np.ascontiguousarray(np.concatenate([wg, wu], axis=0).T)
    wdt = np.ascontiguousarray(wd.T)
    keysl = np.ascontiguousarray(keys.transpose(0, 2, 3, 1).reshape(4, 64, 128))
    kbd = np.zeros((2, 128, 512), np.float32)
    for h in range(4):
        c = h // 2
        for p in range(2):
            base = 64 * (h % 2) + 32 * p
            kbd[c, base:base + 32, ((h % 2) * 2 + p) * 128:((h % 2) * 2 + p + 1) * 128] =                 keysl[h, 32 * p:32 * p + 32, :]

    shared = {"w1t": w1t, "wpt": wpt, "wgut": wgut, "wdt": wdt, "keysl": keysl, "kbd": kbd,
              "de": np.ascontiguousarray(de), "ue": np.ascontiguousarray(ue)}
    shared.update(_consts())
    in_maps = []
    for b in range(NCORES):
        xb = np.ascontiguousarray(
            x[b].reshape(256, 32, D).transpose(1, 0, 2).reshape(T, D))
        m = dict(shared)
        m["x"] = xb
        in_maps.append(m)
    return in_maps


def kernel(**inputs):
    in_maps = _prep_inputs(inputs)
    nc = build()
    res = run_bass_kernel_spmd(nc, in_maps, core_ids=list(range(NCORES)))
    outs = []
    for b in range(NCORES):
        ot = np.asarray(res.results[b]["out"])
        outs.append(ot.reshape(32, 256, D).transpose(1, 0, 2).reshape(T, D))
    return np.stack(outs).astype(np.float32)
